# version 10
# speedup vs baseline: 1.2207x; 1.0017x over previous
import numpy as np
import ml_dtypes
import concourse.bass as bass
import concourse.mybir as mybir
from concourse.bass_utils import run_bass_kernel_spmd
from contextlib import ExitStack

F32 = mybir.dt.float32
BF16 = mybir.dt.bfloat16
AF = mybir.ActivationFunctionType
ALU = mybir.AluOpType
AX = mybir.AxisListType.X

D = 1024
I_FFN = 2816
NCORES = 8
LN_EPS = 1e-5

THR_A = [0.25, 0.75, 1.25, 1.75]
THR_B = [2.5, 3.5, 5.0, 5.0]


def _perm():
    j = np.arange(D)
    return 4 * (j % 256) + (j // 256)


def _col_chunks(W, cmax=512):
    out = []
    off = 0
    while off < W:
        w = min(cmax, W - off)
        out.append((off, w))
        off += w
    return out


class Ctr:
    def __init__(self, handle):
        self.h = handle
        self.n = 0

    def inc(self, instr, k=1):
        instr.then_inc(self.h, k)
        self.n += k
        return self.n


def quant_pre(v, srcap, q, W):
    nb = W // 16
    absx = q["tq"][:, :W]
    v.scalar_tensor_tensor(absx, srcap, -1.0, srcap, ALU.mult,
                           ALU.max)
    v.tensor_reduce(q["m6"][:, :nb],
                    srcap.rearrange("p (a b) -> p a b", b=16),
                    mybir.AxisListType.X, ALU.max,
                    apply_absolute_value=True)
    v.tensor_scalar(q["sg"][:, :W], srcap, 0.0, None, ALU.is_ge)
    v.tensor_scalar(q["sg2"][:, :W], q["sg"][:, :W], 2.0, -1.0,
                    ALU.mult, ALU.add)
    m6b = q["m6"][:, :nb].unsqueeze(-1).broadcast_to([128, nb, 16])
    return v.tensor_scalar(q["ss"][:, :W].rearrange("p (a b) -> p a b", b=16),
                           m6b, 1.0 / 6.0, 1e-12, ALU.mult, ALU.max)


def quant_r_big(sc, q, rb, lntmp, W):
    sc.activation(lntmp[:, :W], q["ss"][:, :W], AF.Ln, bias=0.0, scale=1.0)
    return sc.activation(rb[:, :W], lntmp[:, :W], AF.Exp, bias=0.0,
                         scale=-1.0)

def quant_post(v, srcap, dst, q, rsrc, W):
    nb = W // 16
    s3 = srcap.rearrange("p (a b) -> p a b", b=16)
    v.tensor_scalar(q["s2"][:, :nb], q["m6"][:, :nb], 1.0 / 12.0,
                    None, ALU.mult)
    s2b = q["s2"][:, :nb].unsqueeze(-1).broadcast_to([128, nb, 16])
    v.tensor_tensor(q["th"][:, :W], q["tq"][:, :W], rsrc[:, :W],
                    ALU.mult)
    th = q["th"][:, :W]
    v.tensor_scalar(q["a1"][:, :W], th, THR_A[0], None, ALU.is_gt)
    v.scalar_tensor_tensor(q["a2"][:, :W], th, THR_A[1],
                           q["a1"][:, :W], ALU.is_gt, ALU.add)
    v.scalar_tensor_tensor(q["a3"][:, :W], th, THR_A[2],
                           q["a2"][:, :W], ALU.is_gt, ALU.add)
    v.scalar_tensor_tensor(q["a4"][:, :W], th, THR_A[3],
                           q["a3"][:, :W], ALU.is_gt, ALU.add)
    v.tensor_scalar(q["b1"][:, :W], th, THR_B[0], None, ALU.is_gt)
    v.scalar_tensor_tensor(q["b2"][:, :W], th, THR_B[1],
                           q["b1"][:, :W], ALU.is_gt, ALU.add)
    v.scalar_tensor_tensor(q["b3"][:, :W], th, THR_B[2],
                           q["b2"][:, :W], ALU.is_gt, ALU.add)
    v.scalar_tensor_tensor(q["b4"][:, :W], th, THR_B[3],
                           q["b3"][:, :W], ALU.is_gt, ALU.add)
    v.scalar_tensor_tensor(q["n2"][:, :W], q["b4"][:, :W], 2.0,
                           q["a4"][:, :W], ALU.mult, ALU.add)
    v.tensor_tensor(q["ss"][:, :W].rearrange("p (a b) -> p a b", b=16),
                    q["sg2"][:, :W].rearrange("p (a b) -> p a b", b=16),
                    s2b, ALU.mult)
    return v.tensor_tensor(dst, q["n2"][:, :W], q["ss"][:, :W],
                           ALU.mult)


def build_program(n_tiles, seq_tiles, taps=()):
    T = n_tiles * 128
    taps = set(taps)
    nc = bass.Bass("TRN2", target_bir_lowering=False,
                   detect_race_conditions=False)

    x_d = nc.dram_tensor("x", [T, D], F32, kind="ExternalInput").ap()
    wkT_d = nc.dram_tensor("wkT", [D, D], F32, kind="ExternalInput").ap()
    wvT_d = nc.dram_tensor("wvT", [D, D], F32, kind="ExternalInput").ap()
    woT_d = nc.dram_tensor("woT", [D, D], F32, kind="ExternalInput").ap()
    bk_d = nc.dram_tensor("bk", [1, D], F32, kind="ExternalInput").ap()
    bv_d = nc.dram_tensor("bv", [1, D], F32, kind="ExternalInput").ap()
    bo_d = nc.dram_tensor("bo", [1, D], F32, kind="ExternalInput").ap()
    g_d = nc.dram_tensor("g", [1, D], F32, kind="ExternalInput").ap()
    b_d = nc.dram_tensor("b", [1, D], F32, kind="ExternalInput").ap()
    wg_d = nc.dram_tensor("wg", [I_FFN, D], F32, kind="ExternalInput").ap()
    wu_d = nc.dram_tensor("wu", [I_FFN, D], F32, kind="ExternalInput").ap()
    wd_d = nc.dram_tensor("wd", [D, I_FFN], F32, kind="ExternalInput").ap()
    out_d = nc.dram_tensor("out", [T, D], F32, kind="ExternalOutput").ap()

    x1_d = nc.dram_tensor("x1scr", [T, D], F32, kind="Internal").ap()
    wgT_d = nc.dram_tensor("wgTscr", [D, I_FFN], BF16, kind="Internal").ap()
    wuT_d = nc.dram_tensor("wuTscr", [D, I_FFN], BF16, kind="Internal").ap()
    wdT_d = nc.dram_tensor("wdTscr", [I_FFN, D], BF16, kind="Internal").ap()

    tap_outs = {}

    def tap_out(name, shape, dtype=F32):
        t = nc.dram_tensor("tap_" + name, shape, dtype, kind="ExternalOutput")
        tap_outs[name] = t.ap()
        return tap_outs[name]

    ident_np = np.eye(128, dtype=ml_dtypes.bfloat16)
    tri_np = np.triu(np.ones((128, 128), np.float32)).astype(ml_dtypes.bfloat16)
    ones1_np = np.ones((1, 128), dtype=ml_dtypes.bfloat16)
    onescol_np = np.ones((128, 1), dtype=ml_dtypes.bfloat16)
    ones1f_np = np.ones((1, 128), dtype=np.float32)
    rp_np = np.zeros((128, n_tiles), np.float32)
    for t_ in range(n_tiles):
        pos = (t_ % seq_tiles) * 128 + np.arange(128) + 1.0
        rp_np[:, t_] = 1.0 / np.sqrt(pos)
    ident_c = nc.inline_tensor(ident_np, "c_ident").ap()
    tri_c = nc.inline_tensor(tri_np, "c_tri").ap()
    ones1_c = nc.inline_tensor(ones1_np, "c_ones1").ap()
    onescol_c = nc.inline_tensor(onescol_np, "c_onescol").ap()
    ones1f_c = nc.inline_tensor(ones1f_np, "c_ones1f").ap()
    rp_c = nc.inline_tensor(rp_np, "c_rp").ap()

    NQC = [slice(256 * c, 256 * (c + 1)) for c in range(4)]
    BOUND_SIGNS = [
        [(0, 0, 1), (1, 1, -1), (2, 2, -1), (3, 3, -1)],
        [(0, 1, 1), (1, 0, 1), (2, 3, 1), (3, 2, -1)],
        [(0, 2, 1), (2, 0, 1), (3, 1, 1), (1, 3, -1)],
        [(0, 3, 1), (3, 0, 1), (1, 2, 1), (2, 1, -1)],
    ]
    RETR_SIGNS = [
        [(0, 0, 1), (1, 1, 1), (2, 2, 1), (3, 3, 1)],
        [(1, 0, 1), (0, 1, -1), (3, 2, 1), (2, 3, -1)],
        [(1, 3, 1), (0, 2, -1), (2, 0, 1), (3, 1, -1)],
        [(2, 1, 1), (0, 3, -1), (3, 0, 1), (1, 2, -1)],
    ]

    S0 = ExitStack()
    with S0:
        def sb0(name, shape, dtype=F32):
            return S0.enter_context(nc.sbuf_tensor(name, shape, dtype))

        def sem(name):
            return Ctr(S0.enter_context(nc.semaphore(name)))

        ident_s = sb0("ident_s", [128, 128], BF16)
        tri_s = sb0("tri_s", [128, 128], BF16)
        ones1_s = sb0("ones1_s", [1, 128], BF16)
        onescol_s = sb0("onescol_s", [128, 1], BF16)
        ones1f_s = sb0("ones1f_s", [1, 128], F32)
        rp_s = sb0("rp_s", [128, n_tiles], F32)

        sLx = [sem("sLx0"), sem("sLx1")]
        sSt = [sem("sSt0"), sem("sSt1")]
        sL2 = [sem("sL20"), sem("sL21"), sem("sL22")]
        sSo = [sem("sSo0"), sem("sSo1")]
        sGP = sem("sGP")
        sWL = sem("sWL")
        sPL = [sem("sPL0"), sem("sPL1")]
        sPS = sem("sPS")
        sV = sem("sV")
        sFV = sem("sFV")
        sA = sem("sA")
        sP = sem("sP")
        sG = sem("sG")
        sB = sem("sB")


        S1 = ExitStack()
        with S1:
            def sb1(name, shape, dtype=F32):
                return S1.enter_context(nc.sbuf_tensor(name, shape, dtype))

            def ps1(name, shape, dtype=F32):
                return S1.enter_context(nc.psum_tensor(name, shape, dtype))

            wkT_s = [sb1(f"wkT{k}", [128, D], BF16) for k in range(8)]
            wvT_s = [sb1(f"wvT{k}", [128, D], BF16) for k in range(8)]
            woT_s = [sb1(f"woT{k}", [128, D], BF16) for k in range(8)]
            bkrow = sb1("bkrow", [1, D], BF16)
            bvrow = sb1("bvrow", [1, D], BF16)
            borow = sb1("borow", [1, D], BF16)
            G_rep = sb1("G_rep", [128, D], F32)
            B_rep = sb1("B_rep", [128, D], F32)

            xf = [sb1("xf0", [128, D], F32), sb1("xf1", [128, D], F32)]
            xb16 = sb1("xb16", [128, D], BF16)
            xT = sb1("xT", [128, D], BF16)
            Kf = sb1("Kf", [128, D], F32)
            vb = sb1("vb", [128, D], BF16)
            kh = sb1("kh", [128, D], BF16)
            kq = [sb1(f"kq{i}", [128, 256], F32) for i in range(4)]
            ks01 = sb1("ks01", [128, 256], F32)
            ks23 = sb1("ks23", [128, 256], F32)
            kn2 = sb1("kn2", [128, 256], F32)
            kn2m = sb1("kn2m", [128, 256], F32)
            klns = sb1("klns", [128, 256], F32)
            krn = sb1("krn", [128, 256], F32)
            bound = sb1("bound", [128, D], BF16)
            mem_s = sb1("mem_s", [128, D], BF16)
            qp = [sb1(f"qp{i}", [128, 256], BF16) for i in range(16)]
            qs = [sb1(f"qs{i}", [128, 256], BF16) for i in range(8)]
            retr = sb1("retr", [128, D], BF16)
            sqscr = sb1("sqscr", [128, D], F32)
            sqscr2 = sb1("sqscr2", [128, D], F32)
            t1 = sb1("t1", [128, D], F32)
            ng_t = sb1("ng_t", [128, D], F32)
            normed = sb1("normed", [128, D], BF16)
            normT = sb1("normT", [128, D], BF16)
            yt = sb1("yt", [128, D], F32)
            x1s = [sb1("x1s0", [128, D], F32), sb1("x1s1", [128, D], F32)]
            carry_f = [sb1("carry_f0", [1, D], F32), sb1("carry_f1", [1, D], F32)]
            carry_b = [sb1("carry_b0", [1, D], BF16), sb1("carry_b1", [1, D], BF16)]
            sm = {nm: sb1("sm_" + nm, [128, 1], F32) for nm in
                  ["lns", "rstd", "nmr", "lns1", "rstd1", "nmr1",
                   "vpe", "vpe1", "negmean", "negmean1"]}
            st6 = sb1("st6", [128, 12], F32)
            junk1 = sb1("junk1", [128, 256], F32)
            st2 = sb1("st2", [128, 2], F32)
            st6y = sb1("st6y", [128, 12], F32)
            st2y = sb1("st2y", [128, 2], F32)

            psKO = ps1("psKO", [128, 1024], F32)
            psV = ps1("psV", [128, 1024], F32)
            psM = ps1("psM", [128, 1024], F32)
            psX = ps1("psX", [128, 512], F32)
            psN = ps1("psN", [128, 512], F32)
            psXb = psX.ap().bitcast(BF16)
            psNb = psN.ap().bitcast(BF16)

            S1b = ExitStack()
            def sb1b(name, shape, dtype=F32):
                return S1b.enter_context(nc.sbuf_tensor(name, shape, dtype))
            stagA = [sb1b("stagA0", [128, D], F32), sb1b("stagA1", [128, D], F32)]
            stagF = [sb1b("stagF0", [128, I_FFN], F32),
                     sb1b("stagF1", [128, I_FFN], F32)]
            stag_b = sb1b("stag_b", [1, D], F32)
            wqc = sb1b("wqc", [128, I_FFN], BF16)
            outstag = sb1b("outstag", [128, 1024], BF16)
            _halves = []
            for tns in [xf[0], xf[1], yt, t1, ng_t, sqscr, sqscr2, x1s[0]]:
                _halves.append(tns[:, 0:512])
                _halves.append(tns[:, 512:1024])
            pth, pcA, pcB, pn2 = _halves[0:4]
            ptqa = [_halves[4], _halves[5], _halves[13]]
            psgn = [_halves[6], _halves[7], _halves[14]]
            prbq = [_halves[8], _halves[9], _halves[15]]
            pssb = [_halves[10], _halves[11], Kf[:, 96:608]]
            plntmp = _halves[12]
            pm6 = [Kf[:, 0:32], Kf[:, 32:64], Kf[:, 64:96]]

            def quat_mul(v, a_get, b_get, out, signs):
                last = None
                for c in range(4):
                    terms = signs[c]
                    for t_i, (pa, pb, _) in enumerate(terms):
                        v.tensor_tensor(qp[4 * c + t_i][:], a_get(pa), b_get(pb),
                                        ALU.mult)
                    s0op = ALU.add if terms[1][2] > 0 else ALU.subtract
                    v.tensor_tensor(qs[2 * c][:], qp[4 * c][:],
                                    qp[4 * c + 1][:], s0op)
                    sa, sb_ = terms[2][2], terms[3][2]
                    if sa > 0 and sb_ > 0:
                        v.tensor_tensor(qs[2 * c + 1][:], qp[4 * c + 2][:],
                                        qp[4 * c + 3][:], ALU.add)
                        pairop = ALU.add
                    elif sa > 0 and sb_ < 0:
                        v.tensor_tensor(qs[2 * c + 1][:], qp[4 * c + 2][:],
                                        qp[4 * c + 3][:], ALU.subtract)
                        pairop = ALU.add
                    elif sa < 0 and sb_ > 0:
                        v.tensor_tensor(qs[2 * c + 1][:], qp[4 * c + 3][:],
                                        qp[4 * c + 2][:], ALU.subtract)
                        pairop = ALU.add
                    else:
                        v.tensor_tensor(qs[2 * c + 1][:], qp[4 * c + 2][:],
                                        qp[4 * c + 3][:], ALU.add)
                        pairop = ALU.subtract
                    last = v.tensor_tensor(out[:, NQC[c]], qs[2 * c][:],
                                           qs[2 * c + 1][:], pairop)
                return last

            DVE_P1 = ["cast", "retrprev", "stat1prev", "n2", "normedprev",
                      "khat", "bound", "yprev", "stat2prev", "memevac",
                      "carry"]
            ACT_P1 = ["xtevac", "kevac", "apply1prev", "rnorm", "vevac",
                      "ntevacprev", "x1prev"]
            PE_P1 = ["trX", "kproj", "vproj", "trNprev", "oprojprev",
                     "cumsum", "colsum"]
            base = {}

            def MV(i, name):
                return base["V"] + i * len(DVE_P1) + DVE_P1.index(name) + 1

            def MA(i, name):
                return base["A"] + i * len(ACT_P1) + ACT_P1.index(name) + 1

            def MP(i, name):
                return base["P"] + i * len(PE_P1) + PE_P1.index(name) + 1

            attn_sched = []
            for (dsrc, slist) in [(wkT_d, wkT_s), (wvT_d, wvT_s), (woT_d, woT_s)]:
                for k in range(8):
                    attn_sched.append((dsrc[128 * k:128 * (k + 1), :], slist[k]))
            ffn_sched = []
            for c in range(22):
                ffn_sched.append((wg_d[128 * c:128 * (c + 1), :], D, 8, wgT_d,
                                  128 * c))
            for c in range(22):
                ffn_sched.append((wu_d[128 * c:128 * (c + 1), :], D, 8, wuT_d,
                                  128 * c))
            for c in range(8):
                ffn_sched.append((wd_d[128 * c:128 * (c + 1), :], I_FFN, 22,
                                  wdT_d, 128 * c))
            NA = len(attn_sched)
            NF = len(ffn_sched)
            pcs = []
            for _j, (_d, _H, _ntr, _dd, _c0) in enumerate(ffn_sched):
                for _cc, (_o0, _w) in enumerate(_col_chunks(_H)):
                    pcs.append((_j, _cc, _o0, _w))
            NPC = len(pcs)
            last_pc_of = {}
            for _idx, (_j, _cc, _o0, _w) in enumerate(pcs):
                last_pc_of[_j] = _idx
            prep_qpre = [0] * NPC
            prep_chain = [0] * NPC
            _dv = 3 + NA
            for _pc in range(NPC):
                if _pc >= 2:
                    _dv += 1
                    prep_chain[_pc - 2] = _dv
                _dv += 1
                prep_qpre[_pc] = _dv
            for _pc in (NPC - 2, NPC - 1):
                _dv += 1
                prep_chain[_pc] = _dv
            groups = []
            groups_of = {}
            for j, (_d, H, ntr, dd, c0) in enumerate(ffn_sched):
                groups_of[j] = []
                for grp in range((ntr + 7) // 8):
                    lo, hi = grp * 8, min(grp * 8 + 8, ntr)
                    groups_of[j].append(len(groups))
                    groups.append((j, lo, hi, dd, c0))
            NG = len(groups)
            prep_qr = [0] * NPC
            evac_val = [0] * NG
            _av = 1
            for _j in range(NF):
                _nccs = len(_col_chunks(ffn_sched[_j][1]))
                _bpc = last_pc_of[_j] - _nccs + 1
                for _cc in range(_nccs):
                    _av += 1
                    prep_qr[_bpc + _cc] = _av
                    if _cc == 1 and _j >= 1:
                        for _g in groups_of[_j - 1]:
                            _av += 1
                            evac_val[_g] = _av
            for _g in groups_of[NF - 1]:
                _av += 1
                evac_val[_g] = _av
            prep_pt = [0] * NPC
            prep_pd = [0] * NPC
            _gv = 0
            for _pc in range(NPC):
                _gv += 1
                prep_pt[_pc] = _gv
                _gv += 1
                prep_pd[_pc] = _gv
            pe_done = []
            _pv = 2
            for _j, (_d, _H, _ntr, _dd, _c0) in enumerate(ffn_sched):
                _pv += (_ntr + 7) // 8
                pe_done.append(_pv)
            store_cum = []
            tot = 0
            for (_j, lo, hi, _dd, _c0) in groups:
                tot += (hi - lo) * 16
                store_cum.append(tot)

            with nc.Block("prep") as blk:
                @blk.sync
                def _(sy):
                    for (csrc, cdst) in [(ident_c, ident_s), (tri_c, tri_s),
                                         (ones1_c, ones1_s),
                                         (onescol_c, onescol_s),
                                         (ones1f_c, ones1f_s), (rp_c, rp_s)]:
                        sPL_ = sy.dma_start(out=cdst[:], in_=csrc)
                        sWL.inc(sPL_, 16)
                    sPL[0].inc(sy.dma_start(out=stag_b[:], in_=g_d), 16)
                    sy.wait_ge(sP.h, 1)
                    sPL[0].inc(sy.dma_start(out=stag_b[:], in_=b_d), 16)
                    sy.wait_ge(sP.h, 2)
                    for bi, bsrc in enumerate([bk_d, bv_d, bo_d]):
                        sPL[0].inc(sy.dma_start(out=stag_b[:], in_=bsrc), 16)
                        sy.wait_ge(sV.h, bi + 1)
                    for j, (dsrc, _dst) in enumerate(attn_sched):
                        if j >= 2:
                            sy.wait_ge(sV.h, 3 + (j - 1))
                        sPL[j % 2].inc(
                            sy.dma_start(out=stagA[j % 2][:], in_=dsrc), 16)
                    for j, (dsrc, H, _ntr, _dd, _c0) in enumerate(ffn_sched):
                        if j >= 2:
                            sy.wait_ge(sV.h, prep_qpre[last_pc_of[j - 2]])
                            sy.wait_ge(sA.h, prep_qr[last_pc_of[j - 2]])
                        else:
                            sy.wait_ge(sV.h, 3 + (len(attn_sched) - 2 + j) + 1)
                        sPL[j % 2].inc(
                            sy.dma_start(out=stagF[j % 2][:, :H], in_=dsrc), 16)
                    sB.inc(sy.nop())

                @blk.tensor
                def _(t):
                    t.wait_ge(sWL.h, 96)
                    t.wait_ge(sPL[0].h, 16)
                    for h in range(2):
                        last = t.matmul(psKO[:, 512 * h:512 * (h + 1)],
                                        ones1f_s[:],
                                        stag_b[0:1, 512 * h:512 * (h + 1)],
                                        start=True, stop=True)
                    sP.inc(last)
                    t.wait_ge(sA.h, 1)
                    t.wait_ge(sPL[0].h, 32)
                    for h in range(2):
                        last = t.matmul(psV[:, 512 * h:512 * (h + 1)],
                                        ones1f_s[:],
                                        stag_b[0:1, 512 * h:512 * (h + 1)],
                                        start=True, stop=True)
                    sP.inc(last)
                    for gidx, (j, lo, hi, _dd, _c0) in enumerate(groups):
                        t.wait_ge(sGP.h, prep_pd[last_pc_of[j]])
                        if gidx >= 1:
                            t.wait_ge(sA.h, evac_val[gidx - 1])
                        for kk in range(lo, hi):
                            last = t.matmul(
                                psXb[:, 128 * (kk - lo):128 * (kk - lo + 1)],
                                wqc[:, 128 * kk:128 * (kk + 1)], ident_s[:],
                                is_transpose=True, start=(kk == lo),
                                stop=(kk == hi - 1))
                        sP.inc(last)
                    t.wait_ge(sP.h, sP.n)
                    sB.inc(t.nop())

                @blk.scalar
                def _(sc):
                    sc.wait_ge(sP.h, 1)
                    sA.inc(sc.copy(G_rep[:], psKO.ap()))
                    sc.wait_ge(sP.h, 2)
                    sc.copy(B_rep[:], psV.ap())
                    def p_evac(gidx):
                        lo, hi = groups[gidx][1], groups[gidx][2]
                        sc.wait_ge(sP.h, 2 + gidx + 1)
                        if gidx >= 1:
                            sc.wait_ge(sPS.h, store_cum[gidx - 1])
                        sA.inc(sc.copy(outstag[:, :128 * (hi - lo)],
                                       psXb[:, :128 * (hi - lo)]))

                    pc = 0
                    for j, (_d, H, ntr, _dd2, _c02) in enumerate(ffn_sched):
                        p = j % 2
                        npre = (5 + 12) if p == 0 else 12
                        sc.wait_ge(sPL[p].h, 16 * (npre + j // 2 + 1))
                        for cc, (o0, w) in enumerate(_col_chunks(H)):
                            par = pc % 3
                            if pc >= 3:
                                sc.wait_ge(sGP.h, prep_pd[pc - 3])
                            sc.activation(ptqa[par][:, :w],
                                          stagF[p][:, o0:o0 + w], AF.Abs,
                                          bias=0.0, scale=1.0)
                            sc.activation(psgn[par][:, :w],
                                          stagF[p][:, o0:o0 + w], AF.Sign,
                                          bias=0.0, scale=1.0)
                            sc.wait_ge(sV.h, prep_qpre[pc])
                            sc.activation(plntmp[:, :w], pssb[par][:, :w],
                                          AF.Ln, bias=0.0, scale=1.0)
                            sA.inc(sc.activation(prbq[par][:, :w],
                                                 plntmp[:, :w], AF.Exp,
                                                 bias=0.0, scale=-1.0))
                            pc += 1
                            if cc == 1 and j >= 1:
                                for gidx in groups_of[j - 1]:
                                    p_evac(gidx)
                    for gidx in groups_of[NF - 1]:
                        p_evac(gidx)
                    sc.wait_ge(sA.h, sA.n)
                    sB.inc(sc.nop())

                @blk.vector
                def _(v):
                    v.wait_ge(sWL.h, 96)
                    for bi, brow in enumerate([bkrow, bvrow, borow]):
                        v.wait_ge(sP.h, 2)
                        v.wait_ge(sPL[0].h, 16 * (3 + bi))
                        sV.inc(v.tensor_copy(brow[:], stag_b[:]))
                    for j, (_d, dst) in enumerate(attn_sched):
                        p = j % 2
                        nload = (5 + j // 2 + 1) if p == 0 else (j // 2 + 1)
                        v.wait_ge(sPL[p].h, 16 * nload)
                        sV.inc(v.tensor_copy(dst[:], stagA[j % 2][:]))
                    def p_thA(k):
                        wk = pcs[k][3]
                        par = k % 3
                        th = pth[:, :wk]
                        v.tensor_tensor(th, ptqa[par][:, :wk],
                                        prbq[par][:, :wk], ALU.mult)
                        A, B = pcA[:, :wk], pcB[:, :wk]
                        v.tensor_scalar(A, th, THR_A[0], None, ALU.is_gt)
                        v.scalar_tensor_tensor(B, th, THR_A[1], A,
                                               ALU.is_gt, ALU.add)
                        v.scalar_tensor_tensor(A, th, THR_A[2], B,
                                               ALU.is_gt, ALU.add)
                        v.scalar_tensor_tensor(B, th, THR_A[3], A,
                                               ALU.is_gt, ALU.add)

                    def p_Bn2(k):
                        wk = pcs[k][3]
                        th = pth[:, :wk]
                        A, B = pcA[:, :wk], pcB[:, :wk]
                        N = pn2[:, :wk]
                        v.tensor_scalar(A, th, THR_B[0], None, ALU.is_gt)
                        v.scalar_tensor_tensor(N, th, THR_B[1], A,
                                               ALU.is_gt, ALU.add)
                        v.scalar_tensor_tensor(A, th, THR_B[2], N,
                                               ALU.is_gt, ALU.add)
                        v.scalar_tensor_tensor(N, th, THR_B[3], A,
                                               ALU.is_gt, ALU.add)
                        return v.scalar_tensor_tensor(pcA[:, :wk], B, 0.5,
                                                      N, ALU.mult, ALU.add)

                    pc = 0
                    for j, (_d, H, ntr, _dd, _c0) in enumerate(ffn_sched):
                        p = j % 2
                        npre = (5 + 12) if p == 0 else 12
                        v.wait_ge(sPL[p].h, 16 * (npre + j // 2 + 1))
                        for cc, (o0, w) in enumerate(_col_chunks(H)):
                            par = pc % 3
                            v.tensor_reduce(
                                pm6[par][:, :w // 16],
                                stagF[p][:, o0:o0 + w].rearrange(
                                    "p (a b) -> p a b", b=16),
                                mybir.AxisListType.X, ALU.max,
                                apply_absolute_value=True)
                            if pc >= 2:
                                k = pc - 2
                                v.wait_ge(sA.h, prep_qr[k])
                                if k >= 1:
                                    v.wait_ge(sGP.h, prep_pd[k - 1])
                                p_thA(k)
                                sV.inc(p_Bn2(k))
                            else:
                                v.tensor_copy(pcA[:, 0:128], ident_s[:])
                                v.tensor_copy(pcB[:, 0:128], ident_s[:])
                            nb = w // 16
                            m6b = pm6[par][:, :nb].unsqueeze(-1) \
                                .broadcast_to([128, nb, 16])
                            sV.inc(v.tensor_scalar(
                                pssb[par][:, :w].rearrange(
                                    "p (a b) -> p a b", b=16),
                                m6b, 1.0 / 6.0, 1e-12, ALU.mult,
                                ALU.max))
                            pc += 1
                    for k in (NPC - 2, NPC - 1):
                        v.wait_ge(sA.h, prep_qr[k])
                        v.wait_ge(sGP.h, prep_pd[k - 1])
                        p_thA(k)
                        sV.inc(p_Bn2(k))
                    v.wait_ge(sV.h, sV.n)
                    sB.inc(v.nop())

                @blk.gpsimd
                def _(g):
                    def p_store(gidx):
                        j_, lo, hi, dd, c0 = groups[gidx]
                        g.wait_ge(sA.h, evac_val[gidx])
                        for kk in range(lo, hi):
                            sPS.inc(g.dma_start(
                                out=dd[128 * kk:128 * (kk + 1), c0:c0 + 128],
                                in_=outstag[:, 128 * (kk - lo):128 * (kk - lo + 1)]),
                                16)

                    pc = 0
                    for j, (_d, H, ntr, _dd, _c0) in enumerate(ffn_sched):
                        if j >= 1:
                            for gidx in groups_of[j - 1]:
                                p_store(gidx)
                        for cc, (o0, w) in enumerate(_col_chunks(H)):
                            par = pc % 3
                            g.wait_ge(sV.h, prep_chain[pc])
                            sGP.inc(g.tensor_tensor(
                                pth[:, :w], pcA[:, :w],
                                pssb[par][:, :w], ALU.mult))
                            if cc == 0 and j >= 1:
                                g.wait_ge(sP.h, pe_done[j - 1])
                            sGP.inc(g.tensor_tensor(
                                wqc[:, o0:o0 + w], pth[:, :w],
                                psgn[par][:, :w], ALU.mult))
                            pc += 1
                    for gidx in groups_of[NF - 1]:
                        p_store(gidx)
                    g.wait_ge(sPS.h, sPS.n)
                    sB.inc(g.nop())

            assert sV.n == prep_chain[-1], (sV.n, prep_chain[-1])
            S1b.close()

            base["V"] = sV.n
            base["A"] = sA.n
            base["P"] = sP.n

            with nc.Block("phase1") as blk:
                @blk.sync
                def _(sy):
                    sy.wait_ge(sB.h, 5)
                    for i in range(n_tiles):
                        p = i % 2
                        if i >= 2:
                            sy.wait_ge(sV.h, MV(i - 1, "yprev"))
                        sLx[p].inc(sy.dma_start(
                            out=xf[p][:], in_=x_d[128 * i:128 * (i + 1), :]), 16)
                    sB.inc(sy.nop())

                @blk.tensor
                def _(t):
                    t.wait_ge(sB.h, 5)

                    def trN_oproj(k):
                        t.wait_ge(sV.h, MV(k + 1, "normedprev"))
                        t.wait_ge(sV.h, MV(k, "carry"))
                        for kk in range(8):
                            last = t.matmul(psNb[:, 128 * kk:128 * (kk + 1)],
                                            normed[:, 128 * kk:128 * (kk + 1)],
                                            ident_s[:], is_transpose=True,
                                            start=(kk == 0), stop=(kk == 7))
                        sP.inc(last)
                        t.wait_ge(sA.h, MA(k + 1, "ntevacprev"))
                        t.wait_ge(sA.h, MA(k + 1, "vevac"))
                        for h in range(2):
                            hs = slice(512 * h, 512 * (h + 1))
                            for kk in range(8):
                                t.matmul(psV[:, hs],
                                         normT[:, 128 * kk:128 * (kk + 1)],
                                         woT_s[kk][:, hs], start=(kk == 0),
                                         stop=False)
                            last = t.matmul(psV[:, hs], ones1_s[:],
                                            borow[0:1, hs], start=False,
                                            stop=True)
                        sP.inc(last)

                    for i in range(n_tiles):
                        i0 = i % seq_tiles
                        t.wait_ge(sV.h, MV(i, "cast"))
                        if i >= 1:
                            t.wait_ge(sA.h, MA(i - 1, "xtevac"))
                            t.wait_ge(sV.h, MV(i - 1, "carry"))
                        for k in range(8):
                            last = t.matmul(psXb[:, 128 * k:128 * (k + 1)],
                                            xb16[:, 128 * k:128 * (k + 1)],
                                            ident_s[:], is_transpose=True,
                                            start=(k == 0), stop=(k == 7))
                        sP.inc(last)
                        t.wait_ge(sA.h, MA(i, "xtevac"))
                        if i >= 1:
                            t.wait_ge(sA.h, MA(i - 1, "kevac"))
                        for h in range(2):
                            hs = slice(512 * h, 512 * (h + 1))
                            for k in range(8):
                                t.matmul(psKO[:, hs],
                                         xT[:, 128 * k:128 * (k + 1)],
                                         wkT_s[k][:, hs], start=(k == 0),
                                         stop=False)
                            last = t.matmul(psKO[:, hs], ones1_s[:],
                                            bkrow[0:1, hs], start=False,
                                            stop=True)
                        sP.inc(last)
                        if i >= 2:
                            t.wait_ge(sV.h, MV(i - 1, "yprev"))
                        for h in range(2):
                            hs = slice(512 * h, 512 * (h + 1))
                            for k in range(8):
                                t.matmul(psV[:, hs],
                                         xT[:, 128 * k:128 * (k + 1)],
                                         wvT_s[k][:, hs], start=(k == 0),
                                         stop=False)
                            last = t.matmul(psV[:, hs], ones1_s[:],
                                            bvrow[0:1, hs], start=False,
                                            stop=True)
                        sP.inc(last)
                        if i >= 1:
                            trN_oproj(i - 1)
                        else:
                            for _ in range(2):
                                sP.inc(t.matmul(
                                    psNb[:, 0:128], ident_s[:], ident_s[:],
                                    is_transpose=True, start=True, stop=True))
                        t.wait_ge(sV.h, MV(i, "bound"))
                        if i >= 1:
                            t.wait_ge(sV.h, MV(i - 1, "memevac"))
                        for h in range(2):
                            hs = slice(512 * h, 512 * (h + 1))
                            last = t.matmul(psM[:, hs], tri_s[:], bound[:, hs],
                                            start=True, stop=(i0 == 0))
                            if i0 != 0:
                                last = t.matmul(psM[:, hs], ones1_s[:],
                                                carry_b[i % 2][0:1, hs],
                                                start=False, stop=True)
                        sP.inc(last)
                        if i >= 1:
                            t.wait_ge(sA.h, MA(i, "ntevacprev"))
                        t.matmul(psX[0:1, :], onescol_s[:], bound[:, 0:512],
                                 start=True, stop=True)
                        last = t.matmul(psN[0:1, :], onescol_s[:],
                                        bound[:, 512:1024], start=True,
                                        stop=True)
                        sP.inc(last)
                    t.wait_ge(sV.h, MV(n_tiles - 1, "carry"))
                    for _ in range(3):
                        sP.inc(t.matmul(
                            psXb[:, 0:128], ident_s[:], ident_s[:],
                            is_transpose=True, start=True, stop=True))
                    trN_oproj(n_tiles - 1)
                    sB.inc(t.nop())

                @blk.vector
                def _(v):
                    v.wait_ge(sB.h, 5)

                    def retr_stat1(k):
                        v.wait_ge(sV.h, sV.n)
                        v.wait_ge(sA.h, MA(k, "apply1prev"))
                        last = quat_mul(v, lambda c: mem_s[:, NQC[c]],
                                        lambda c: kh[:, NQC[c]], retr,
                                        RETR_SIGNS)
                        sV.inc(last)
                        v.bn_stats(st6[:, 0:6], retr[:, 0:512])
                        sFV.inc(v.bn_stats(st6[:, 6:12], retr[:, 512:1024]))
                        v.wait_ge(sFV.h, sFV.n)
                        sFV.inc(v.bn_aggr(st2[:], st6[:]))
                        v.wait_ge(sFV.h, sFV.n)
                        v.tensor_scalar(sm["vpe"][:], st2[:, 1:2],
                                        LN_EPS, None, ALU.add)
                        v.tensor_scalar(sm["negmean"][:], st2[:, 0:1],
                                        -1.0, None, ALU.mult)
                        sV.inc(v.tensor_scalar(junk1[:], retr[:, 0:256], 1.0,
                                               None, ALU.mult))

                    def normed_(k):
                        v.wait_ge(sA.h, MA(k + 1, "apply1prev"))
                        if k >= 1:
                            v.wait_ge(sP.h, MP(k, "trNprev"))
                        v.tensor_tensor(ng_t[:], t1[:], G_rep[:], ALU.mult)
                        sV.inc(v.tensor_tensor(normed[:], ng_t[:], B_rep[:],
                                               ALU.add))

                    def y_stat2(k):
                        v.wait_ge(sP.h, MP(k + 1, "oprojprev"))
                        if k >= 1:
                            v.wait_ge(sA.h, MA(k, "x1prev"))
                        sV.inc(v.scalar_tensor_tensor(
                            yt[:], xf[k % 2][:], 2.0, psV.ap(), ALU.mult,
                            ALU.add))
                        v.bn_stats(st6y[:, 0:6], yt[:, 0:512])
                        sFV.inc(v.bn_stats(st6y[:, 6:12], yt[:, 512:1024]))
                        v.wait_ge(sFV.h, sFV.n)
                        sFV.inc(v.bn_aggr(st2y[:], st6y[:]))
                        v.wait_ge(sFV.h, sFV.n)
                        v.tensor_scalar(sm["vpe1"][:], st2y[:, 1:2],
                                        LN_EPS, None, ALU.add)
                        v.tensor_scalar(sm["negmean1"][:], st2y[:, 0:1],
                                        -1.0, None, ALU.mult)
                        sV.inc(v.tensor_scalar(junk1[:], yt[:, 0:256], 1.0,
                                               None, ALU.mult))

                    for i in range(n_tiles):
                        p = i % 2
                        i0 = i % seq_tiles
                        if i >= 2:
                            v.wait_ge(sV.h, MV(i - 1, "stat2prev"))
                        if i >= 1:
                            v.wait_ge(sP.h, MP(i - 1, "trX"))
                        v.wait_ge(sLx[p].h, 16 * (i // 2 + 1))
                        sV.inc(v.tensor_copy(xb16[:], xf[p][:]))
                        if i >= 1:
                            retr_stat1(i - 1)
                        else:
                            sV.inc(v.tensor_copy(junk1[:, 0:128],
                                                 ident_s[:]))
                            sV.inc(v.tensor_copy(junk1[:, 0:128],
                                                 ident_s[:]))
                        v.wait_ge(sA.h, MA(i, "kevac"))
                        for c in range(4):
                            v.tensor_tensor(kq[c][:], Kf[:, NQC[c]],
                                            Kf[:, NQC[c]], ALU.mult)
                        v.tensor_tensor(ks01[:], kq[0][:], kq[1][:], ALU.add)
                        v.tensor_tensor(ks23[:], kq[2][:], kq[3][:], ALU.add)
                        v.tensor_tensor(kn2[:], ks01[:], ks23[:], ALU.add)
                        sV.inc(v.tensor_scalar(kn2m[:], kn2[:], 1e-24, None,
                                               ALU.max))
                        if i >= 1:
                            normed_(i - 1)
                        else:
                            sV.inc(v.tensor_copy(junk1[:, 0:128],
                                                 ident_s[:]))
                        v.wait_ge(sA.h, MA(i, "rnorm"))
                        for c in range(4):
                            last = v.tensor_tensor(kh[:, NQC[c]], Kf[:, NQC[c]],
                                                   krn[:], ALU.mult)
                        sV.inc(last)
                        v.wait_ge(sA.h, MA(i, "vevac"))
                        if i >= 1:
                            v.wait_ge(sP.h, MP(i - 1, "colsum"))
                        last = quat_mul(v, lambda c: vb[:, NQC[c]],
                                        lambda c: kh[:, NQC[c]], bound,
                                        BOUND_SIGNS)
                        sV.inc(last)
                        if i >= 1:
                            y_stat2(i - 1)
                        else:
                            sV.inc(v.tensor_copy(junk1[:, 0:128],
                                                 ident_s[:]))
                            sV.inc(v.tensor_copy(junk1[:, 0:128],
                                                 ident_s[:]))
                        v.wait_ge(sP.h, MP(i, "cumsum"))
                        sV.inc(v.tensor_scalar(mem_s[:], psM.ap(),
                                               rp_s[:, i:i + 1], None,
                                               ALU.mult))
                        v.wait_ge(sP.h, MP(i, "colsum"))
                        pn = (i + 1) % 2
                        if i0 == 0:
                            v.tensor_copy(carry_f[pn][0:1, 0:512], psX[0:1, :])
                            v.tensor_copy(carry_f[pn][0:1, 512:1024],
                                          psN[0:1, :])
                        else:
                            v.scalar_tensor_tensor(carry_f[pn][0:1, 0:512],
                                                   psX[0:1, :], 0.0,
                                                   carry_f[i % 2][0:1, 0:512],
                                                   ALU.bypass, ALU.add)
                            v.scalar_tensor_tensor(carry_f[pn][0:1, 512:1024],
                                                   psN[0:1, :], 0.0,
                                                   carry_f[i % 2][0:1, 512:1024],
                                                   ALU.bypass, ALU.add)
                        v.tensor_copy(carry_b[pn][:], carry_f[pn][:])
                        sV.inc(v.tensor_scalar(junk1[:],
                                               mem_s[:, 0:256], 1.0, None,
                                               ALU.mult))
                    i = n_tiles
                    sV.inc(v.tensor_copy(junk1[:, 0:128],
                                         ident_s[:]))
                    retr_stat1(i - 1)
                    sV.inc(v.tensor_copy(junk1[:, 0:128],
                                         ident_s[:]))
                    normed_(i - 1)
                    sV.inc(v.tensor_copy(junk1[:, 0:128],
                                         ident_s[:]))
                    sV.inc(v.tensor_copy(junk1[:, 0:128],
                                         ident_s[:]))
                    y_stat2(i - 1)
                    v.wait_ge(sV.h, sV.n)
                    sB.inc(v.nop())

                @blk.scalar
                def _(sc):
                    sc.wait_ge(sB.h, 5)

                    def apply1_(k):
                        sc.wait_ge(sV.h, MV(k + 1, "stat1prev"))
                        sc.activation(sm["lns"][:], sm["vpe"][:], AF.Ln,
                                      bias=0.0, scale=1.0)
                        sc.activation(sm["rstd"][:], sm["lns"][:],
                                      AF.Exp, bias=0.0, scale=-0.5)
                        if k >= 1:
                            sc.wait_ge(sV.h, MV(k, "normedprev"))
                        sc.activation(sqscr[:], retr[:], AF.Identity,
                                      bias=sm["negmean"][:], scale=1.0)
                        sA.inc(sc.activation(t1[:], sqscr[:], AF.Identity,
                                             bias=0.0,
                                             scale=sm["rstd"][:]))

                    def ntevac_(k):
                        sc.wait_ge(sP.h, MP(k + 1, "trNprev"))
                        sA.inc(sc.copy(normT[:], psNb[:]))

                    def x1_(k):
                        sc.wait_ge(sV.h, MV(k + 1, "stat2prev"))
                        sc.activation(sm["lns1"][:], sm["vpe1"][:], AF.Ln,
                                      bias=0.0, scale=1.0)
                        sc.activation(sm["rstd1"][:], sm["lns1"][:],
                                      AF.Exp, bias=0.0, scale=-0.5)
                        sc.activation(sqscr2[:], yt[:], AF.Identity,
                                      bias=sm["negmean1"][:], scale=1.0)
                        if k >= 2:
                            sc.wait_ge(sSt[k % 2].h, 16 * (k // 2))
                        sA.inc(sc.activation(x1s[k % 2][:], sqscr2[:],
                                             AF.Identity, bias=0.0,
                                             scale=sm["rstd1"][:]))

                    def act_dummy():
                        sA.inc(sc.activation(sm["lns"][:], onescol_s[:],
                                             AF.Identity, bias=0.0,
                                             scale=1.0))

                    for i in range(n_tiles):
                        sc.wait_ge(sP.h, MP(i, "trX"))
                        sA.inc(sc.copy(xT[:], psXb[:]))
                        sc.wait_ge(sP.h, MP(i, "kproj"))
                        if i >= 1:
                            sc.wait_ge(sV.h, MV(i - 1, "khat"))
                        sA.inc(sc.copy(Kf[:], psKO.ap()))
                        if i >= 1:
                            apply1_(i - 1)
                        else:
                            act_dummy()
                        sc.wait_ge(sV.h, MV(i, "n2"))
                        sc.activation(klns[:], kn2m[:], AF.Ln, bias=0.0,
                                      scale=1.0)
                        sA.inc(sc.activation(krn[:], klns[:], AF.Exp, bias=0.0,
                                             scale=-0.5))
                        sc.wait_ge(sP.h, MP(i, "vproj"))
                        if i >= 1:
                            sc.wait_ge(sV.h, MV(i - 1, "bound"))
                        sA.inc(sc.copy(vb[:], psV.ap()))
                        if i >= 1:
                            ntevac_(i - 1)
                            x1_(i - 1)
                        else:
                            act_dummy()
                            act_dummy()
                    i = n_tiles
                    act_dummy()
                    act_dummy()
                    apply1_(i - 1)
                    act_dummy()
                    act_dummy()
                    ntevac_(i - 1)
                    x1_(i - 1)
                    sc.wait_ge(sA.h, sA.n)
                    sB.inc(sc.nop())

                @blk.gpsimd
                def _(g):
                    g.wait_ge(sB.h, 5)
                    for i in range(n_tiles):
                        p = i % 2
                        g.wait_ge(sA.h, MA(i + 1, "x1prev"))
                        sSt[p].inc(g.dma_start(
                            out=x1_d[128 * i:128 * (i + 1), :], in_=x1s[p][:]),
                            16)
                    g.wait_ge(sSt[0].h, sSt[0].n)
                    g.wait_ge(sSt[1].h, sSt[1].n)
                    sB.inc(g.nop())

            P1_TAPS = {"Kf": (Kf, F32), "kh": (kh, BF16), "vb": (vb, BF16),
                       "bound": (bound, BF16), "mem_s": (mem_s, BF16),
                       "retr": (retr, BF16), "normed": (normed, BF16),
                       "yt": (yt, F32), "G_rep": (G_rep, F32),
                       "B_rep": (B_rep, F32), "t1": (t1, F32)}
            P1_SMALL = {"mean": st2[:, 0:1], "var": st2[:, 1:2],
                        "rstd": sm["rstd"], "nmr": sm["nmr"]}

            p1taps = [nm for nm in taps if nm in P1_TAPS]
            if p1taps or ("x1" in taps) or any(t.startswith("sm_") for t in taps):
                with nc.Block("tap1") as blk:
                    @blk.gpsimd
                    def _(g):
                        g.wait_ge(sB.h, 10)
                        for nm in p1taps:
                            buf, dt_ = P1_TAPS[nm]
                            o = tap_out(nm, [128, D], F32)
                            if dt_ == BF16:
                                g.tensor_copy(sqscr[:], buf[:])
                                sG.inc(g.dma_start(out=o, in_=sqscr[:]), 16)
                            else:
                                sG.inc(g.dma_start(out=o, in_=buf[:]), 16)
                            g.wait_ge(sG.h, sG.n)
                        for nm, buf in P1_SMALL.items():
                            if "sm_" + nm not in taps:
                                continue
                            o = tap_out("sm_" + nm, [128, 1], F32)
                            sG.inc(g.dma_start(out=o, in_=buf[:]), 16)
                            g.wait_ge(sG.h, sG.n)
                        if "x1" in taps:
                            o = tap_out("x1", [T, D], F32)
                            sG.inc(g.dma_start(out=o, in_=x1_d), 16)
                        g.wait_ge(sG.h, sG.n)


        S2 = ExitStack()
        with S2:
            def sb2(name, shape, dtype=F32):
                return S2.enter_context(nc.sbuf_tensor(name, shape, dtype))

            def ps2(name, shape, dtype=F32):
                return S2.enter_context(nc.psum_tensor(name, shape, dtype))

            wgT_s = [sb2(f"wgTs{k}", [128, I_FFN], BF16) for k in range(8)]
            wuT_s = [sb2(f"wuTs{k}", [128, I_FFN], BF16) for k in range(8)]
            wdT_s = [sb2(f"wdTs{k}", [128, D], BF16) for k in range(22)]
            x1b = [sb2(f"x1b{k}", [128, D], F32) for k in range(3)]
            xq = sb2("xq", [128, D], BF16)
            xqT = sb2("xqT", [128, D], BF16)
            q2 = {"m6": [sb2("q2_m6a", [128, 32], F32),
                         sb2("q2_m6b", [128, 32], F32)]}
            for nm in ["th", "cA", "cB", "n2"]:
                q2[nm] = sb2("q2_" + nm, [128, 512], F32)
            tsc = sb2("tsc", [128, 512], F32)
            tqa = [sb2("tqa0", [128, 512], F32), sb2("tqa1", [128, 512], F32)]
            sgn = [sb2("sgn0", [128, 512], F32), sb2("sgn1", [128, 512], F32)]
            rbq = [sb2("rbq0", [128, 512], F32), sb2("rbq1", [128, 512], F32)]
            ssb = [sb2("ssb0", [128, 512], F32), sb2("ssb1", [128, 512], F32)]
            hcb = [sb2("hcb0", [128, 512], F32), sb2("hcb1", [128, 512], F32)]
            esc = sb2("esc", [128, 512], F32)
            spsc = sb2("spsc", [128, 512], F32)
            sgb = [sb2("sgb0", [128, 512], F32), sb2("sgb1", [128, 512], F32)]
            ubb = [sb2("ubb0", [128, 512], F32), sb2("ubb1", [128, 512], F32)]
            hqb = [sb2("hqb0", [128, 512], BF16), sb2("hqb1", [128, 512], BF16)]
            hqT = [sb2("hqT0", [128, 512], BF16), sb2("hqT1", [128, 512], BF16)]
            zt = sb2("zt", [128, D], F32)
            outb = sb2("outb", [128, D], F32)
            s2m = {nm: sb2("s2_" + nm, [128, 1], F32) for nm in
                   ["vpe2", "negmean2", "lnz", "rstd2"]}
            st6z = sb2("st6z", [128, 12], F32)
            st2z = sb2("st2z", [128, 2], F32)

            psG = [ps2("psG0", [128, 512], F32), ps2("psG1", [128, 512], F32)]
            psU = [ps2("psU0", [128, 512], F32), ps2("psU1", [128, 512], F32)]
            psD = ps2("psD", [128, 1024], F32)
            psT = ps2("psT", [128, 512], F32)
            psT2 = ps2("psT2", [128, 512], F32)
            psTb = psT.ap().bitcast(BF16)
            psT2b = psT2.ap().bitcast(BF16)

            ICH = _col_chunks(I_FFN)
            NCH = len(ICH)
            assert NCH == 6

            DVE_PRE = ["pqpxa", "pgnxa", "pqpxb", "pgnxb"]
            ACT_PRE = ["pqra", "pqrb", "pxqevac"]
            GP_PRE = ["pgtxa", "pgdxa", "pgtxb", "pgdxb"]
            DVE_P2 = ["ap0", "zprev", "statzprev", "qp0",
                      "ap1", "gn0", "qp1", "ap2", "gn1", "qp2",
                      "ap3", "gn2", "qp3", "gn3", "qpxa", "gnxa", "qpxb",
                      "ap4", "gnxb", "qp4", "ap5", "gn4", "qp5", "gn5"]
            ACT_P2 = ["su0", "su1", "qr0",
                      "su2", "qr1", "outprev", "su3", "qr2", "hqTe0",
                      "su4", "qr3", "hqTe1", "qrxa", "su5", "qrxb",
                      "qr4", "hqTe2", "qr5", "xqevacn", "hqTe3",
                      "hqTe4", "hqTe5"]
            GP_P2 = ["hcb0", "hcb1", "gt0", "gd0", "hcb2", "gt1", "gd1",
                     "hcb3", "gt2", "gd2", "gt3", "gd3", "gtxa", "gdxa",
                     "hcb4", "gtxb", "gdxb", "hcb5", "gt4", "gd4",
                     "gt5", "gd5"]
            PE_PRE = ["ptrxq"]
            PE_P2 = ["gu0", "gu1", "gu2", "gu3", "trhq0", "gu4", "trhq1",
                     "gu5", "trhq2", "trxqnext", "trhq3", "trhq4", "trhq5",
                     "down"]
            base2 = {"V": sV.n, "A": sA.n, "P": sP.n, "G": sGP.n}

            def MV2(i, name):
                if i < 0:
                    return base2["V"] + DVE_PRE.index(name) + 1
                return (base2["V"] + len(DVE_PRE) + i * len(DVE_P2)
                        + DVE_P2.index(name) + 1)

            def MA2(i, name):
                if i < 0:
                    return base2["A"] + ACT_PRE.index(name) + 1
                return (base2["A"] + len(ACT_PRE) + i * len(ACT_P2)
                        + ACT_P2.index(name) + 1)

            def MG2(i, name):
                if i < 0:
                    return base2["G"] + GP_PRE.index(name) + 1
                return (base2["G"] + len(GP_PRE) + i * len(GP_P2)
                        + GP_P2.index(name) + 1)

            def MP2(i, name):
                if i < 0:
                    return base2["P"] + PE_PRE.index(name) + 1
                return (base2["P"] + len(PE_PRE) + i * len(PE_P2)
                        + PE_P2.index(name) + 1)

            def LD2(i):
                return 16 * (i // 3 + 1)

            with nc.Block("phase2") as blk:
                @blk.sync
                def _(sy):
                    sy.wait_ge(sB.h, 10)
                    for k in range(8):
                        sWL.inc(sy.dma_start(
                            out=wgT_s[k][:],
                            in_=wgT_d[128 * k:128 * (k + 1), :]), 16)
                    for k in range(8):
                        sWL.inc(sy.dma_start(
                            out=wuT_s[k][:],
                            in_=wuT_d[128 * k:128 * (k + 1), :]), 16)
                    for k in range(22):
                        sWL.inc(sy.dma_start(
                            out=wdT_s[k][:],
                            in_=wdT_d[128 * k:128 * (k + 1), :]), 16)
                    for i in range(n_tiles):
                        if i >= 3:
                            sy.wait_ge(sA.h, MA2(i - 2, "outprev"))
                            sy.wait_ge(sV.h, MV2(i - 2, "zprev"))
                        sL2[i % 3].inc(sy.dma_start(
                            out=x1b[i % 3][:],
                            in_=x1_d[128 * i:128 * (i + 1), :]), 16)

                @blk.vector
                def _(v):
                    v.wait_ge(sB.h, 10)

                    def chain_thA(g_tq, g_rb, w):
                        th = q2["th"][:, :w]
                        v.tensor_tensor(th, g_tq[:, :w], g_rb[:, :w],
                                        ALU.mult)
                        A, B = q2["cA"][:, :w], q2["cB"][:, :w]
                        v.tensor_scalar(A, th, THR_A[0], None, ALU.is_gt)
                        v.scalar_tensor_tensor(B, th, THR_A[1], A,
                                               ALU.is_gt, ALU.add)
                        v.scalar_tensor_tensor(A, th, THR_A[2], B,
                                               ALU.is_gt, ALU.add)
                        v.scalar_tensor_tensor(B, th, THR_A[3], A,
                                               ALU.is_gt, ALU.add)

                    def chain_Bn2(w):
                        th = q2["th"][:, :w]
                        A, B = q2["cA"][:, :w], q2["cB"][:, :w]
                        N = q2["n2"][:, :w]
                        v.tensor_scalar(A, th, THR_B[0], None, ALU.is_gt)
                        v.scalar_tensor_tensor(N, th, THR_B[1], A,
                                               ALU.is_gt, ALU.add)
                        v.scalar_tensor_tensor(A, th, THR_B[2], N,
                                               ALU.is_gt, ALU.add)
                        v.scalar_tensor_tensor(N, th, THR_B[3], A,
                                               ALU.is_gt, ALU.add)
                        return v.scalar_tensor_tensor(q2["cA"][:, :w], B, 0.5,
                                                      N, ALU.mult, ALU.add)

                    def reduce_(src, par, w):
                        return v.tensor_reduce(
                            q2["m6"][par][:, :w // 16],
                            src.rearrange("p (a b) -> p a b", b=16),
                            mybir.AxisListType.X, ALU.max,
                            apply_absolute_value=True)

                    def ssb_(par, w):
                        nb = w // 16
                        m6b = q2["m6"][par][:, :nb].unsqueeze(-1) \
                            .broadcast_to([128, nb, 16])
                        return v.tensor_scalar(
                            ssb[par][:, :w].rearrange("p (a b) -> p a b",
                                                      b=16),
                            m6b, 1.0 / 6.0, 1e-12, ALU.mult, ALU.max)

                    v.wait_ge(sL2[0].h, LD2(0))
                    reduce_(x1b[0][:, 0:512], 0, 512)
                    v.tensor_copy(q2["cA"][:, 0:128], ident_s[:])
                    v.tensor_copy(q2["cB"][:, 0:128], ident_s[:])
                    sV.inc(ssb_(0, 512))
                    reduce_(x1b[0][:, 512:1024], 1, 512)
                    v.wait_ge(sA.h, MA2(-1, "pqra"))
                    chain_thA(tqa[0], rbq[0], 512)
                    sV.inc(chain_Bn2(512))
                    sV.inc(ssb_(1, 512))
                    v.wait_ge(sA.h, MA2(-1, "pqrb"))
                    v.wait_ge(sGP.h, MG2(-1, "pgdxa"))
                    chain_thA(tqa[1], rbq[1], 512)
                    sV.inc(chain_Bn2(512))

                    for i in range(n_tiles):
                        last_tile = (i == n_tiles - 1)
                        def hq_slot(c, prev_thA, gn_name):
                            par = c % 2
                            o0, w = ICH[c]
                            v.wait_ge(sA.h, MA2(i, f"su{c}"))
                            sV.inc(v.tensor_tensor(
                                tsc[:, :w], sgb[par][:, :w],
                                psG[par][:, :w], ALU.mult))
                            if prev_thA is not None:
                                ppar, pw, qr_n, gd_i, gd_n = prev_thA
                                v.wait_ge(sA.h, MA2(i, qr_n))
                                if gd_n is not None:
                                    v.wait_ge(sGP.h, MG2(gd_i, gd_n))
                                chain_thA(tqa[ppar], rbq[ppar], pw)
                            v.wait_ge(sGP.h, MG2(i, f"hcb{c}"))
                            reduce_(hcb[par][:, :w], par, w)
                            if prev_thA is not None:
                                sV.inc(chain_Bn2(prev_thA[1]))
                            ssb_after = sV.inc(ssb_(par, w))

                        c, (o0, w) = 0, ICH[0]
                        v.wait_ge(sA.h, MA2(i, "su0"))
                        sV.inc(v.tensor_tensor(
                            tsc[:, :w], sgb[0][:, :w],
                            psG[0][:, :w], ALU.mult))
                        if i >= 1:
                            v.wait_ge(sP.h, MP2(i - 1, "down"))
                            if i >= 2:
                                v.wait_ge(sA.h, MA2(i - 1, "outprev"))
                            pz = (i - 1) % 3
                            sV.inc(v.scalar_tensor_tensor(
                                zt[:], psD.ap(), 0.0, x1b[pz][:],
                                ALU.bypass, ALU.add))
                            v.bn_stats(st6z[:, 0:6], zt[:, 0:512])
                            sFV.inc(v.bn_stats(st6z[:, 6:12],
                                               zt[:, 512:1024]))
                        else:
                            sV.inc(v.tensor_copy(
                                zt[:, 0:128], ident_s[:]))
                        v.wait_ge(sGP.h, MG2(i, "hcb0"))
                        reduce_(hcb[0][:, :w], 0, w)
                        if i >= 1:
                            v.wait_ge(sFV.h, sFV.n)
                            sFV.inc(v.bn_aggr(st2z[:], st6z[:]))
                            v.wait_ge(sFV.h, sFV.n)
                            v.tensor_scalar(s2m["vpe2"][:], st2z[:, 1:2],
                                            LN_EPS, None, ALU.add)
                            v.tensor_scalar(s2m["negmean2"][:],
                                            st2z[:, 0:1], -1.0, None,
                                            ALU.mult)
                            v.tensor_scalar(q2["cB"][:, 0:256],
                                            zt[:, 0:256], 1.0, None,
                                            ALU.mult)
                            sV.inc(v.tensor_scalar(
                                q2["cB"][:, 0:256], zt[:, 256:512],
                                1.0, None, ALU.mult))
                        else:
                            v.tensor_copy(zt[:, 0:128], ident_s[:])
                            sV.inc(v.tensor_copy(
                                zt[:, 128:256], ident_s[:]))
                        sV.inc(ssb_(0, w))
                        hq_slot(1, (0, ICH[0][1], "qr0",
                                    i - 1 if i >= 1 else -1,
                                    "gd5" if i >= 1 else "pgdxb"), "gn0")
                        hq_slot(2, (1, ICH[1][1], "qr1", i, "gd0"), "gn1")
                        hq_slot(3, (0, ICH[2][1], "qr2", i, "gd1"), "gn2")
                        v.wait_ge(sA.h, MA2(i, "qr3"))
                        v.wait_ge(sGP.h, MG2(i, "gd2"))
                        chain_thA(tqa[1], rbq[1], ICH[3][1])
                        if not last_tile:
                            v.wait_ge(sL2[(i + 1) % 3].h, LD2(i + 1))
                            reduce_(x1b[(i + 1) % 3][:, 0:512], 0, 512)
                        sV.inc(chain_Bn2(ICH[3][1]))
                        if not last_tile:
                            sV.inc(ssb_(0, 512))
                            v.wait_ge(sA.h, MA2(i, "qrxa"))
                            v.wait_ge(sGP.h, MG2(i, "gd3"))
                            chain_thA(tqa[0], rbq[0], 512)
                            reduce_(x1b[(i + 1) % 3][:, 512:1024], 1, 512)
                            sV.inc(chain_Bn2(512))
                            sV.inc(ssb_(1, 512))
                        else:
                            for _ in range(3):
                                sV.inc(v.tensor_copy(zt[:, 0:128],
                                                     ident_s[:]))
                        par = 0
                        o0, w = ICH[4]
                        v.wait_ge(sA.h, MA2(i, "su4"))
                        sV.inc(v.tensor_tensor(
                            tsc[:, :w], sgb[0][:, :w],
                            psG[0][:, :w], ALU.mult))
                        if not last_tile:
                            v.wait_ge(sA.h, MA2(i, "qrxb"))
                            v.wait_ge(sGP.h, MG2(i, "gdxa"))
                            chain_thA(tqa[1], rbq[1], 512)
                        v.wait_ge(sGP.h, MG2(i, "hcb4"))
                        reduce_(hcb[0][:, :w], 0, w)
                        if not last_tile:
                            sV.inc(chain_Bn2(512))
                        else:
                            v.tensor_scalar(zt[:, 0:256], zt[:, 256:512],
                                            1.0, None, ALU.mult)
                            sV.inc(v.tensor_scalar(zt[:, 0:256],
                                                   zt[:, 256:512], 1.0,
                                                   None, ALU.mult))
                        sV.inc(ssb_(0, w))
                        hq_slot(5, (0, ICH[4][1], "qr4", i,
                                    "gdxb" if not last_tile else "gd3"),
                                "gn4")
                        v.wait_ge(sA.h, MA2(i, "qr5"))
                        v.wait_ge(sGP.h, MG2(i, "gd4"))
                        chain_thA(tqa[1], rbq[1], ICH[5][1])
                        sV.inc(chain_Bn2(ICH[5][1]))
                    i = n_tiles
                    v.wait_ge(sP.h, MP2(i - 1, "down"))
                    v.wait_ge(sA.h, MA2(i - 1, "outprev"))
                    sV.inc(v.tensor_copy(q2["cA"][:, 0:128],
                                         ident_s[:]))
                    pz = (i - 1) % 3
                    sV.inc(v.scalar_tensor_tensor(
                        zt[:], psD.ap(), 0.0, x1b[pz][:],
                        ALU.bypass, ALU.add))
                    v.bn_stats(st6z[:, 0:6], zt[:, 0:512])
                    sFV.inc(v.bn_stats(st6z[:, 6:12], zt[:, 512:1024]))
                    v.wait_ge(sFV.h, sFV.n)
                    sFV.inc(v.bn_aggr(st2z[:], st6z[:]))
                    v.wait_ge(sFV.h, sFV.n)
                    v.tensor_scalar(s2m["vpe2"][:], st2z[:, 1:2], LN_EPS,
                                    None, ALU.add)
                    v.tensor_scalar(s2m["negmean2"][:], st2z[:, 0:1], -1.0,
                                    None, ALU.mult)
                    v.tensor_scalar(q2["cB"][:, 0:256], zt[:, 0:256], 1.0,
                                    None, ALU.mult)
                    sV.inc(v.tensor_scalar(q2["cB"][:, 0:256],
                                           zt[:, 256:512], 1.0, None,
                                           ALU.mult))
                    v.wait_ge(sV.h, sV.n)
                    sB.inc(v.nop())

                @blk.scalar
                def _(sc):
                    sc.wait_ge(sB.h, 10)

                    def abs_sign(src, par, w):
                        sc.activation(tqa[par][:, :w], src, AF.Abs,
                                      bias=0.0, scale=1.0)
                        return sc.activation(sgn[par][:, :w], src, AF.Sign,
                                             bias=0.0, scale=1.0)

                    def recip(par, w):
                        sc.activation(esc[:, :w], ssb[par][:, :w], AF.Ln,
                                      bias=0.0, scale=1.0)
                        return sc.activation(rbq[par][:, :w], esc[:, :w],
                                             AF.Exp, bias=0.0, scale=-1.0)

                    sc.wait_ge(sL2[0].h, LD2(0))
                    abs_sign(x1b[0][:, 0:512], 0, 512)
                    sc.wait_ge(sV.h, MV2(-1, "pqpxa"))
                    sA.inc(recip(0, 512))
                    abs_sign(x1b[0][:, 512:1024], 1, 512)
                    sc.wait_ge(sV.h, MV2(-1, "pqpxb"))
                    sA.inc(recip(1, 512))
                    sc.wait_ge(sP.h, MP2(-1, "ptrxq"))
                    sA.inc(sc.copy(xqT[:], psTb[:]))

                    for i in range(n_tiles):

                        def emit_hqTe(cc):
                            wcc = ICH[cc][1]
                            sc.wait_ge(sP.h, MP2(i, f"trhq{cc}"))
                            srcb_ = psT2b if cc % 2 == 0 else psTb
                            sA.inc(sc.copy(hqT[cc % 2][:, :wcc],
                                           srcb_[:, :wcc]))

                        def ln_tail():
                            if i >= 1:
                                sc.wait_ge(sV.h, MV2(i, "statzprev"))
                                sc.activation(s2m["lnz"][:], s2m["vpe2"][:],
                                              AF.Ln, bias=0.0, scale=1.0)
                                sc.activation(s2m["rstd2"][:], s2m["lnz"][:],
                                              AF.Exp, bias=0.0, scale=-0.5)
                                sc.activation(x1b[(i - 1) % 3][:], zt[:],
                                              AF.Identity,
                                              bias=s2m["negmean2"][:],
                                              scale=1.0)
                                if i >= 2:
                                    sc.wait_ge(sSo[(i - 2) % 2].h,
                                               16 * ((i - 2) // 2 + 1))
                                sA.inc(sc.activation(
                                    outb[:], x1b[(i - 1) % 3][:],
                                    AF.Identity, bias=0.0,
                                    scale=s2m["rstd2"][:]))
                            else:
                                sA.inc(sc.activation(
                                    s2m["lnz"][:], onescol_s[:], AF.Identity,
                                    bias=0.0, scale=1.0))

                        def silu_su(k):
                            wk = ICH[k][1]
                            pk = k % 2
                            sc.wait_ge(sP.h, MP2(i, f"gu{k}"))
                            if k >= 2:
                                sc.wait_ge(sV.h, MV2(i, f"ap{k - 2}"))
                            elif i >= 1:
                                sc.wait_ge(sV.h, MV2(i - 1, f"ap{4 + k}"))
                            sc.activation(esc[:, :wk], psG[pk][:, :wk],
                                          AF.Exp, bias=0.0, scale=-1.0)
                            sc.activation(spsc[:, :wk], esc[:, :wk], AF.Ln,
                                          bias=1.0, scale=1.0)
                            sc.activation(sgb[pk][:, :wk], spsc[:, :wk],
                                          AF.Exp, bias=0.0, scale=-1.0)
                            if k >= 2:
                                sc.wait_ge(sGP.h, MG2(i, f"hcb{k - 2}"))
                            elif i >= 1:
                                sc.wait_ge(sGP.h, MG2(i - 1, f"hcb{4 + k}"))
                            sA.inc(sc.copy(ubb[pk][:, :wk],
                                           psU[pk][:, :wk]))

                        def absgn_exc(c):
                            wc = ICH[c][1]
                            pc_ = c % 2
                            sc.wait_ge(sGP.h, MG2(i, f"hcb{c}"))
                            abs_sign(hcb[pc_][:, :wc], pc_, wc)
                            sc.wait_ge(sV.h, MV2(i, f"qp{c}"))
                            sA.inc(recip(pc_, wc))

                        silu_su(0)
                        silu_su(1)
                        absgn_exc(0)
                        silu_su(2)
                        absgn_exc(1)
                        ln_tail()
                        silu_su(3)
                        absgn_exc(2)
                        emit_hqTe(0)
                        silu_su(4)
                        absgn_exc(3)
                        emit_hqTe(1)
                        if i < n_tiles - 1:
                            sc.wait_ge(sL2[(i + 1) % 3].h, LD2(i + 1))
                            sc.wait_ge(sGP.h, MG2(i, "gd2"))
                            abs_sign(x1b[(i + 1) % 3][:, 0:512], 0, 512)
                            sc.wait_ge(sV.h, MV2(i, "qpxa"))
                            sA.inc(recip(0, 512))
                        else:
                            sA.inc(sc.activation(
                                s2m["lnz"][:], s2m["vpe2"][:], AF.Identity,
                                bias=0.0, scale=1.0))
                        silu_su(5)
                        if i < n_tiles - 1:
                            sc.wait_ge(sGP.h, MG2(i, "gd3"))
                            abs_sign(x1b[(i + 1) % 3][:, 512:1024], 1, 512)
                            sc.wait_ge(sV.h, MV2(i, "qpxb"))
                            sA.inc(recip(1, 512))
                        else:
                            sA.inc(sc.activation(
                                s2m["lnz"][:], s2m["vpe2"][:], AF.Identity,
                                bias=0.0, scale=1.0))
                        absgn_exc(4)
                        emit_hqTe(2)
                        absgn_exc(5)
                        if i < n_tiles - 1:
                            sc.wait_ge(sP.h, MP2(i, "trxqnext"))
                            sA.inc(sc.copy(xqT[:], psTb[:]))
                        else:
                            sA.inc(sc.activation(
                                s2m["lnz"][:], onescol_s[:], AF.Identity,
                                bias=0.0, scale=1.0))
                        emit_hqTe(3)
                        emit_hqTe(4)
                        emit_hqTe(5)
                    i = n_tiles
                    for _ in range(5):
                        sA.inc(sc.activation(s2m["lnz"][:], onescol_s[:],
                                             AF.Identity, bias=0.0,
                                             scale=1.0))
                    sc.wait_ge(sV.h, MV2(i, "statzprev"))
                    sc.activation(s2m["lnz"][:], s2m["vpe2"][:], AF.Ln,
                                  bias=0.0, scale=1.0)
                    sc.activation(s2m["rstd2"][:], s2m["lnz"][:], AF.Exp,
                                  bias=0.0, scale=-0.5)
                    sc.activation(x1b[(i - 1) % 3][:], zt[:], AF.Identity,
                                  bias=s2m["negmean2"][:], scale=1.0)
                    if i >= 2:
                        sc.wait_ge(sSo[(i - 2) % 2].h,
                                   16 * ((i - 2) // 2 + 1))
                    sA.inc(sc.activation(
                        outb[:], x1b[(i - 1) % 3][:], AF.Identity,
                        bias=0.0, scale=s2m["rstd2"][:]))
                    sc.wait_ge(sA.h, sA.n)
                    sB.inc(sc.nop())

                @blk.tensor
                def _(t):
                    t.wait_ge(sB.h, 10)
                    t.wait_ge(sWL.h, sWL.n)

                    def emit_trxq(war_evac_i):
                        if war_evac_i is not None:
                            t.wait_ge(sA.h, MA2(war_evac_i, "hqTe1"))
                        for k in range(8):
                            last = t.matmul(psTb[:, 128 * k:128 * (k + 1)],
                                            xq[:, 128 * k:128 * (k + 1)],
                                            ident_s[:], is_transpose=True,
                                            start=(k == 0), stop=(k == 7))
                        return last

                    t.wait_ge(sGP.h, MG2(-1, "pgdxb"))
                    sP.inc(emit_trxq(None))
                    for i in range(n_tiles):
                        def emit_gu(c):
                            o0, w = ICH[c]
                            if c >= 2:
                                t.wait_ge(sA.h, MA2(i, f"su{c - 2}"))
                                t.wait_ge(sV.h, MV2(i, f"qp{c - 2}"))
                            elif i >= 1:
                                t.wait_ge(sA.h, MA2(i - 1, f"su{NCH - 2 + c}"))
                                t.wait_ge(sV.h, MV2(i - 1, f"qp{NCH - 2 + c}"))
                            for k in range(8):
                                t.matmul(psG[c % 2][:, :w],
                                         xqT[:, 128 * k:128 * (k + 1)],
                                         wgT_s[k][:, o0:o0 + w],
                                         start=(k == 0), stop=(k == 7))
                            for k in range(8):
                                last = t.matmul(psU[c % 2][:, :w],
                                                xqT[:, 128 * k:128 * (k + 1)],
                                                wuT_s[k][:, o0:o0 + w],
                                                start=(k == 0), stop=(k == 7))
                            sP.inc(last)

                        def emit_trhq_down(c):
                            o0, w = ICH[c]
                            t.wait_ge(sGP.h, MG2(i, f"gd{c}"))
                            if c in (3, 5) and i < n_tiles - 1:
                                t.wait_ge(sA.h, MA2(i, "xqevacn"))
                            elif c >= 2:
                                t.wait_ge(sA.h, MA2(i, f"hqTe{c - 2}"))
                            elif i >= 1:
                                t.wait_ge(sA.h, MA2(i - 1, f"hqTe{NCH - 2 + c}"))
                            elif c == 1:
                                t.wait_ge(sA.h, MA2(-1, "pxqevac"))
                            nsub = w // 128
                            dstb = psT2b if c % 2 == 0 else psTb
                            for kk in range(nsub):
                                last = t.matmul(
                                    dstb[:, 128 * kk:128 * (kk + 1)],
                                    hqb[c % 2][:, 128 * kk:128 * (kk + 1)],
                                    ident_s[:], is_transpose=True,
                                    start=(kk == 0), stop=(kk == nsub - 1))
                            sP.inc(last)
                            t.wait_ge(sA.h, MA2(i, f"hqTe{c}"))
                            for kk in range(nsub):
                                kglob = o0 // 128 + kk
                                for h in range(2):
                                    hs = slice(512 * h, 512 * (h + 1))
                                    last = t.matmul(
                                        psD[:, hs],
                                        hqT[c % 2][:, 128 * kk:128 * (kk + 1)],
                                        wdT_s[kglob][:, hs],
                                        start=(c == 0 and kk == 0),
                                        stop=(c == NCH - 1 and kk == nsub - 1))
                            return last

                        t.wait_ge(sA.h, MA2(i - 1, "xqevacn")
                                  if i >= 1 else MA2(-1, "pxqevac"))
                        for c in range(4):
                            emit_gu(c)
                        t.wait_ge(sV.h, MV2(i, "zprev"))
                        last = emit_trhq_down(0)
                        emit_gu(4)
                        last = emit_trhq_down(1)
                        emit_gu(5)
                        last = emit_trhq_down(2)
                        if i < n_tiles - 1:
                            t.wait_ge(sGP.h, MG2(i, "gdxb"))
                            sP.inc(emit_trxq(i))
                        else:
                            sP.inc(t.matmul(
                                psTb[:, 0:128], ident_s[:], ident_s[:],
                                is_transpose=True, start=True, stop=True))
                        last = emit_trhq_down(3)
                        last = emit_trhq_down(4)
                        last = emit_trhq_down(5)
                        sP.inc(last)
                    sB.inc(t.nop())

                @blk.gpsimd
                def _(g):
                    g.wait_ge(sB.h, 10)

                    def gp_td(gn_val, sspar, sgnpar, dst, w, war=None):
                        g.wait_ge(sV.h, gn_val)
                        t_ = g.tensor_tensor(q2["th"][:, :w],
                                             q2["cA"][:, :w],
                                             ssb[sspar][:, :w], ALU.mult)
                        sGP.inc(t_)
                        if war is not None:
                            g.wait_ge(sP.h, war)
                        sGP.inc(g.tensor_tensor(dst, q2["th"][:, :w],
                                                sgn[sgnpar][:, :w],
                                                ALU.mult))
                    gp_td(MV2(-1, "pgnxa"), 0, 0, xq[:, 0:512], 512)
                    gp_td(MV2(-1, "pgnxb"), 1, 1, xq[:, 512:1024], 512)

                    for i in range(n_tiles):
                        last_tile = (i == n_tiles - 1)

                        def gp_hcb(c):
                            par_ = c % 2
                            w_ = ICH[c][1]
                            g.wait_ge(sV.h, MV2(i, f"ap{c}"))
                            sGP.inc(g.tensor_tensor(
                                hcb[par_][:, :w_], tsc[:, :w_],
                                ubb[par_][:, :w_], ALU.mult))

                        trxq_war = (MP2(i - 1, "trxqnext") if i >= 1
                                    else MP2(-1, "ptrxq"))
                        gp_hcb(0)
                        gp_hcb(1)
                        gp_td(MV2(i, "gn0"), 0, 0, hqb[0][:, :ICH[0][1]],
                              ICH[0][1],
                              war=MP2(i - 1, "trhq4") if i >= 1 else None)
                        if i >= 1:
                            g.wait_ge(sA.h, MA2(i, "outprev"))
                            sSo[(i - 1) % 2].inc(g.dma_start(
                                out=out_d[128 * (i - 1):128 * i, :],
                                in_=outb[:]), 16)
                        gp_hcb(2)
                        gp_td(MV2(i, "gn1"), 1, 1, hqb[1][:, :ICH[1][1]],
                              ICH[1][1],
                              war=MP2(i - 1, "trhq5") if i >= 1 else None)
                        gp_hcb(3)
                        gp_td(MV2(i, "gn2"), 0, 0, hqb[0][:, :ICH[2][1]],
                              ICH[2][1], war=MP2(i, "trhq0"))
                        gp_td(MV2(i, "gn3"), 1, 1, hqb[1][:, :ICH[3][1]],
                              ICH[3][1], war=MP2(i, "trhq1"))
                        if not last_tile:
                            gp_td(MV2(i, "gnxa"), 0, 0, xq[:, 0:512], 512,
                                  war=trxq_war)
                        else:
                            for _ in range(2):
                                sGP.inc(g.tensor_copy(xq[:, 0:128],
                                                      ident_s[:]))
                        gp_hcb(4)
                        if not last_tile:
                            gp_td(MV2(i, "gnxb"), 1, 1, xq[:, 512:1024],
                                  512, war=trxq_war)
                        else:
                            for _ in range(2):
                                sGP.inc(g.tensor_copy(xq[:, 0:128],
                                                      ident_s[:]))
                        gp_hcb(5)
                        gp_td(MV2(i, "gn4"), 0, 0, hqb[0][:, :ICH[4][1]],
                              ICH[4][1], war=MP2(i, "trhq2"))
                        gp_td(MV2(i, "gn5"), 1, 1, hqb[1][:, :ICH[5][1]],
                              ICH[5][1], war=MP2(i, "trhq3"))
                    i = n_tiles
                    g.wait_ge(sA.h, MA2(i, "outprev"))
                    sSo[(i - 1) % 2].inc(g.dma_start(
                        out=out_d[128 * (i - 1):128 * i, :],
                        in_=outb[:]), 16)
                    g.wait_ge(sSo[0].h, sSo[0].n)
                    g.wait_ge(sSo[1].h, sSo[1].n)

            P2_TAPS = {"xq": (xq, BF16), "zt": (zt, F32),
                       "xqT": (xqT, BF16),
                       "sgb1": (sgb[1], F32), "ubb1": (ubb[1], F32),
                       "hqb1": (hqb[1], BF16), "hqT1": (hqT[1], BF16),
                       "hcb": (hcb[1], F32), "rbq": (rbq[1], F32),
                       "q2m6": (q2["m6"][1], F32),
                       "q2th": (q2["th"], F32), "q2tq": (tqa[1], F32),
                       "q2n2": (q2["n2"], F32), "q2ss": (ssb[1], F32)}
            p2taps = [nm for nm in taps if nm in P2_TAPS]
            if p2taps:
                with nc.Block("tap2") as blk:
                    @blk.gpsimd
                    def _(g):
                        g.wait_ge(sB.h, 13)
                        if True:
                            for nm, dsrc in [("wuT0", wuT_d), ("wgT0", wgT_d),
                                             ("wdT0", wdT_d)]:
                                o = tap_out(nm, [128, 1024], F32)
                                sG.inc(g.dma_start(out=o,
                                                   in_=dsrc[0:128, 0:1024]),
                                       16)
                                g.wait_ge(sG.h, sG.n)
                        for nm in p2taps:
                            buf, dt_ = P2_TAPS[nm]
                            o = tap_out(nm, [128, buf[:].shape[1]], F32)
                            if dt_ == BF16:
                                w_ = buf[:].shape[1]
                                g.tensor_copy(x1b[0][:, :w_], buf[:])
                                sG.inc(g.dma_start(out=o, in_=x1b[0][:, :w_]),
                                       16)
                            else:
                                sG.inc(g.dma_start(out=o, in_=buf[:]), 16)
                            g.wait_ge(sG.h, sG.n)

    return nc



def _host_arrays(Wk, bk, Wv, bv, ln_g, ln_b, Wo, bo, w_gate, w_up, w_down):
    perm = _perm()
    f = np.float32
    return {
        "wkT": np.ascontiguousarray(np.asarray(Wk, f)[perm].T),
        "wvT": np.ascontiguousarray(np.asarray(Wv, f)[perm].T),
        "woT": np.ascontiguousarray(np.asarray(Wo, f)[:, perm].T),
        "bk": np.ascontiguousarray(np.asarray(bk, f)[perm].reshape(1, D)),
        "bv": np.ascontiguousarray(np.asarray(bv, f)[perm].reshape(1, D)),
        "bo": np.ascontiguousarray(np.asarray(bo, f).reshape(1, D)),
        "g": np.ascontiguousarray(np.asarray(ln_g, f)[perm].reshape(1, D)),
        "b": np.ascontiguousarray(np.asarray(ln_b, f)[perm].reshape(1, D)),
        "wg": np.ascontiguousarray(np.asarray(w_gate, f)),
        "wu": np.ascontiguousarray(np.asarray(w_up, f)),
        "wd": np.ascontiguousarray(np.asarray(w_down, f)),
    }


def kernel(L0_attn_x, L0_attn_Wk, L0_attn_bk, L0_attn_Wv, L0_attn_bv,
           L0_attn_ln_g, L0_attn_ln_b, L0_attn_Wo, L0_attn_bo,
           L0_ffn_w_gate, L0_ffn_w_up, L0_ffn_w_down):
    x = np.ascontiguousarray(np.asarray(L0_attn_x, np.float32))
    B, L, _D = x.shape
    per_core = B // NCORES
    n_tiles = per_core * (L // 128)
    seq_tiles = L // 128
    arrs = _host_arrays(L0_attn_Wk, L0_attn_bk, L0_attn_Wv, L0_attn_bv,
                        L0_attn_ln_g, L0_attn_ln_b, L0_attn_Wo, L0_attn_bo,
                        L0_ffn_w_gate, L0_ffn_w_up, L0_ffn_w_down)
    nc = build_program(n_tiles, seq_tiles)
    in_maps = []
    for c in range(NCORES):
        m = dict(arrs)
        m["x"] = np.ascontiguousarray(
            x[c * per_core:(c + 1) * per_core].reshape(n_tiles * 128, D))
        in_maps.append(m)
    res = run_bass_kernel_spmd(nc, in_maps, core_ids=list(range(NCORES)))
    out = np.empty((B, L, D), np.float32)
    for c in range(NCORES):
        out[c * per_core:(c + 1) * per_core] = \
            res.results[c]["out"].reshape(per_core, L, D)
    return out
